# version 10
# baseline (speedup 1.0000x reference)
import numpy as np
import concourse.bass as bass
import concourse.mybir as mybir
from concourse.bass_utils import run_bass_kernel_spmd
from contextlib import ExitStack

F32 = mybir.dt.float32
AF = mybir.ActivationFunctionType
ALU = mybir.AluOpType

D = 1024
S = 2048
DCH = 8
KCH = 16
P1B = 512
P2B = 1024
P3B = 512
FF = 4096
FCH = 32
EPS = 1e-5
ESH = -16.0
SCALE = 0.125
NSEQ = 4
NCORES = 8


class IR:
    ENGS = ("sync", "vector", "scalar", "tensor", "gpsimd")

    def __init__(self):
        self.q = {e: [] for e in self.ENGS}
        self.cnt = {}
        self.maxwait = {}

    def op(self, eng, fn, waits=(), inc=None, dma=False):
        n = 16 if dma else 1
        waits = tuple((s, v) for s, v in waits if v > 0)
        for s, v in waits:
            self.maxwait[s] = max(self.maxwait.get(s, 0), v)
        if inc is not None:
            self.cnt[inc] = self.cnt.get(inc, 0) + n
        self.q[eng].append((fn, waits, inc, n))
        return self.cnt.get(inc, 0)

    def c(self, sem):
        return self.cnt.get(sem, 0)

    def validate(self):
        for s, mx in self.maxwait.items():
            assert mx <= self.cnt.get(s, 0), f"sem {s}: wait {mx} > total {self.cnt.get(s,0)}"

    def replay(self, eng_name, eng, sems, boot_sem):
        import os as _os
        if _os.environ.get("BASS_SIM_FP64") != "1" and _os.environ.get("KBOOT") != "0":
            if eng_name == "gpsimd":
                nums = sorted(s.num for s in sems.values())
                rng = range(min(nums), max(nums) + 1)
                eng.dma_reset(rng)
                eng.sem_clear(rng)
                eng.sem_inc(boot_sem, 1)
            else:
                for _ in range(100):
                    eng.nop(cycle_cnt=255)
                eng.wait_ge(boot_sem, 1)
        for fn, waits, inc, n in self.q[eng_name]:
            for s, v in waits:
                eng.wait_ge(sems[s], v)
            inst = fn(eng)
            if inc is not None:
                inst.then_inc(sems[inc], n)


class DmaChan:

    def __init__(self, ir, name, eng):
        self.ir, self.name, self.eng = ir, name, eng

    def dma(self, fn, waits=()):
        w = list(waits)
        prev = self.ir.c(self.name)
        if prev > 0:
            w.append((self.name, prev))
        return self.ir.op(self.eng, fn, waits=w, inc=self.name, dma=True)

    @property
    def count(self):
        return self.ir.c(self.name)


def build(nseq=NSEQ):
    nc = bass.Bass("TRN2", target_bir_lowering=False, debug=False,
                   detect_race_conditions=False)
    x = nc.dram_tensor("x", [nseq * S, D], F32, kind="ExternalInput").ap()
    w_qkv = nc.dram_tensor("w_qkv", [D, 3 * D], F32, kind="ExternalInput").ap()
    b_qkv = nc.dram_tensor("b_qkv", [128, 24], F32, kind="ExternalInput").ap()
    w_proj = nc.dram_tensor("w_proj", [D, D], F32, kind="ExternalInput").ap()
    b_proj = nc.dram_tensor("b_proj", [128, D], F32, kind="ExternalInput").ap()
    w_fc = nc.dram_tensor("w_fc", [FF, D], F32, kind="ExternalInput").ap()
    w_pf = nc.dram_tensor("w_pf", [D, FF], F32, kind="ExternalInput").ap()
    ident = nc.dram_tensor("ident", [128, 128], F32, kind="ExternalInput").ap()
    ones = nc.dram_tensor("ones", [128, 1], F32, kind="ExternalInput").ap()
    out = nc.dram_tensor("out", [nseq * S, D], F32, kind="ExternalOutput").ap()
    xn_d = nc.dram_tensor("xn_d", [D, S], F32, kind="Internal").ap()
    q_d = nc.dram_tensor("q_d", [D, S], F32, kind="Internal").ap()
    k_d = nc.dram_tensor("k_d", [D, S], F32, kind="Internal").ap()
    v_d = nc.dram_tensor("v_d", [S, D], F32, kind="Internal").ap()
    wfcT_d = nc.dram_tensor("wfcT_d", [D, FF], F32, kind="Internal").ap()
    wpfT_d = nc.dram_tensor("wpfT_d", [FF, D], F32, kind="Internal").ap()

    ir = IR()
    with ExitStack() as es:
        def sb(name, shape):
            return es.enter_context(nc.sbuf_tensor(name, shape, F32))
        A1 = sb("A1", [128, 16384])
        A2 = sb("A2", [128, 8192])
        A4 = sb("A4", [128, 8192])
        idt = sb("idt", [128, 128]); bq = sb("bq", [128, 24]); bp = sb("bp", [128, D])
        onescol = sb("onescol", [128, 1])
        xt = sb("xt", [128, D]); c_t = sb("c_t", [128, D])
        sq_t = sb("sq_t", [128, D]); xn_t = sb("xn_t", [128, D])
        mcol = sb("mcol", [128, 1]); vcol = sb("vcol", [128, 1]); vcol2 = sb("vcol2", [128, 1])
        rcol = sb("rcol", [128, 1]); rcol2 = sb("rcol2", [128, 1])
        wbufA = sb("wbufA", [128, DCH * 128]); wbufB = sb("wbufB", [128, DCH * 128])
        wbufC = sb("wbufC", [128, DCH * 128])
        vbufA = sb("vbufA", [128, D]); vbufB = sb("vbufB", [128, D])
        vbufC = sb("vbufC", [128, D])
        stagA = sb("stagA", [128, P1B]); stagB = sb("stagB", [128, P1B])
        tta = sb("tta", [128, 512]); c1a = sb("c1a", [128, 512])
        c2a = sb("c2a", [128, 512]); c3a = sb("c3a", [128, 512])
        ttb = sb("ttb", [128, 512]); c1b = sb("c1b", [128, 512])
        c2b = sb("c2b", [128, 512]); c3b = sb("c3b", [128, 512])
        vst2A = c1a; vst2B = c1b
        ostg = sb("ostg", [128, D])
        ytstA = sb("ytstA", [128, P3B]); ytstB = sb("ytstB", [128, P3B])
        rden = sb("rden", [128, 8])
        partA = sb("partA", [128, D])
        partB = xn_t

        psum = es.enter_context(nc.psum_tensor("psum", [128, 4096], F32))
        PB = lambda b: psum[:, b * 512:(b + 1) * 512]

        ch_a = DmaChan(ir, "ch_a", "sync")
        ch_b = DmaChan(ir, "ch_b", "sync")
        ch_c = DmaChan(ir, "ch_c", "sync")
        ch_d = DmaChan(ir, "ch_d", "sync")
        ch_o1 = DmaChan(ir, "ch_o1", "gpsimd")
        ch_o2 = DmaChan(ir, "ch_o2", "gpsimd")
        chans = [ch_a, ch_b, ch_c, ch_d, ch_o1, ch_o2]

        fm_src = lambda dram, c0, c1: dram[:, c0:c1].rearrange("(c p) f -> p c f", p=128)
        WCH3 = [ch_b, ch_c, ch_d]

        ch_b.dma(lambda e: e.dma_start(out=idt[:], in_=ident[:, :]))
        ch_b.dma(lambda e: e.dma_start(out=bq[:], in_=b_qkv[:, :]))
        ch_b.dma(lambda e: e.dma_start(out=bp[:], in_=b_proj[:, :]))
        ch_b.dma(lambda e: e.dma_start(out=onescol[:], in_=ones[:, :]))

        ev_prev = {0: 0, 1: 0}
        for fc in range(FCH):
            ld = ch_b.dma(lambda e, fc=fc: e.dma_start(out=xt[:], in_=w_fc[fc * 128:(fc + 1) * 128, :]),
                          waits=[("tv", ir.c("tv"))])
            for d in range(DCH):
                t_at = ir.op("tensor", lambda e, d=d: e.transpose(
                    PB(d % 2)[:, 0:128], xt[:, d * 128:(d + 1) * 128], idt[:]),
                    waits=[("ch_b", ld), ("vv", ev_prev[d % 2])], inc="tv")
                ev_prev[d % 2] = ir.op("vector", lambda e, d=d: e.tensor_copy(
                    sq_t[:, d * 128:(d + 1) * 128], PB(d % 2)[:, 0:128]),
                    waits=[("tv", t_at), ("ch_o1", ch_o1.count)], inc="vv")
            ch_o1.dma(lambda e, fc=fc: e.dma_start(
                out=fm_src(wfcT_d, fc * 128, (fc + 1) * 128),
                in_=sq_t[:].rearrange("p (c f) -> p c f", f=128)),
                waits=[("vv", ir.c("vv"))])
        for dm in range(DCH):
            ld = ch_b.dma(lambda e, dm=dm: e.dma_start(out=A1[:, 0:FF], in_=w_pf[dm * 128:(dm + 1) * 128, :]),
                          waits=[("tv", ir.c("tv"))])
            for g in range(4):
                for j in range(8):
                    fc = g * 8 + j
                    t_at = ir.op("tensor", lambda e, fc=fc: e.transpose(
                        PB(fc % 2)[:, 0:128], A1[:, fc * 128:(fc + 1) * 128], idt[:]),
                        waits=[("ch_b", ld), ("vv", ev_prev[fc % 2])], inc="tv")
                    ev_prev[fc % 2] = ir.op("vector", lambda e, j=j, fc=fc: e.tensor_copy(
                        xn_t[:, j * 128:(j + 1) * 128], PB(fc % 2)[:, 0:128]),
                        waits=[("tv", t_at), ("ch_o1", ch_o1.count)], inc="vv")
                f0 = g * 8
                ch_o1.dma(lambda e, dm=dm, f0=f0: e.dma_start(
                    out=wpfT_d[f0 * 128:(f0 + 8) * 128, dm * 128:(dm + 1) * 128].rearrange(
                        "(c p) f -> p c f", p=128),
                    in_=xn_t[:].rearrange("p (c f) -> p c f", f=128)),
                    waits=[("vv", ir.c("vv"))])

        prep_o1 = ch_o1.count
        prep_tv = ir.c("tv")
        WB = [wbufA, wbufB, wbufC]
        VB = [vbufA, vbufB, vbufC]
        WCH = None
        wbuf_g = {0: 0, 1: 0, 2: 0}
        vbuf_g = {0: 0, 1: 0, 2: 0}
        a4_g = {0: ir.c("tv"), 1: ir.c("tv")}
        ln_tr_ev = {0: ev_prev[0], 1: ev_prev[1]}
        ln_tr_last = ir.c("tv")
        last_qkv_mm = ir.c("tv")
        last_proj_mm = ir.c("tv")
        p3_pass1_last = ir.c("tv")
        exp_last = 0
        last_xn_sv = 0
        last_ct_vv = 0
        last_vcol2_vv = 0
        last_recip_vv = 0
        last_id_sv = 0
        yevac_vv = [0, 0, 0, 0]

        for seq in range(nseq):
            xs = x[seq * S:(seq + 1) * S, :]
            os_ = out[seq * S:(seq + 1) * S, :]
            qkv_ev = {0: 0, 2: 0}
            vtr_ev = {1: 0, 3: 0}
            st_q = {0: ch_o1.count, 1: ch_o1.count}
            st_v = {0: ch_o2.count, 1: ch_o2.count}
            for blk in range(S // P1B):
                half = (blk % 2) * (DCH * P1B)
                for t in range(P1B // 128):
                    r0 = blk * P1B + t * 128
                    first = (seq == 0 and blk == 0 and t == 0)
                    ld = ch_a.dma(lambda e, r=r0, xs=xs: e.dma_start(out=xt[:], in_=xs[r:r + 128, :]),
                                  waits=[("vv", last_ct_vv), ("sv", last_id_sv)]
                                        + ([("tv", prep_tv)] if first else []))
                    last_id_sv = ir.op("scalar", lambda e: e.activation(
                        sq_t[:], xt[:], AF.Identity, scale=np.float32(1.0 / D), accum_out=mcol[:]),
                        waits=[("ch_a", ld), ("vv", last_ct_vv)]
                              + ([("ch_o1", prep_o1)] if first else []), inc="sv")
                    last_ct_vv = ir.op("vector", lambda e: e.tensor_scalar(
                        c_t[:], xt[:], mcol[:, 0:1], None, op0=ALU.subtract),
                        waits=[("sv", max(last_id_sv, last_xn_sv))], inc="vv")
                    sv_sq = ir.op("scalar", lambda e: e.activation(
                        sq_t[:], c_t[:], AF.Square, scale=np.float32(1.0 / 32.0), accum_out=vcol[:]),
                        waits=[("vv", max(last_ct_vv, last_vcol2_vv))], inc="sv")
                    last_vcol2_vv = ir.op("vector", lambda e: e.tensor_scalar(
                        vcol2[:], vcol[:], EPS, None, op0=ALU.add),
                        waits=[("sv", sv_sq)], inc="vv")
                    sv_sqrt = ir.op("scalar", lambda e: e.activation(rcol2[:], vcol2[:], AF.Sqrt),
                                    waits=[("vv", max(last_vcol2_vv, last_recip_vv))], inc="sv")
                    last_recip_vv = ir.op("vector", lambda e: e.reciprocal(rcol[:], rcol2[:]),
                                          waits=[("sv", max(sv_sqrt, last_xn_sv))], inc="vv")
                    last_xn_sv = ir.op("scalar", lambda e: e.activation(
                        xn_t[:], c_t[:], AF.Identity, scale=rcol[:, 0:1]),
                        waits=[("vv", last_recip_vv), ("tv", ln_tr_last)]
                              + ([("ch_o1", prep_o1)] if first else []), inc="sv")
                    for c in range(DCH):
                        bk = 6 + c % 2
                        t_at = ir.op("tensor", lambda e, c=c, bk=bk: e.transpose(
                            PB(bk)[:, 0:128], xn_t[:, c * 128:(c + 1) * 128], idt[:]),
                            waits=[("sv", last_xn_sv), ("vv", ln_tr_ev[c % 2])], inc="tv")
                        if c == DCH - 1:
                            ln_tr_last = t_at
                        ln_tr_ev[c % 2] = ir.op("vector", lambda e, c=c, t=t, half=half, bk=bk: e.tensor_copy(
                            A4[:, half + c * P1B + t * 128: half + c * P1B + (t + 1) * 128],
                            PB(bk)[:, 0:128]),
                            waits=[("tv", max(t_at, a4_g[(half > 0) * 1])), ("ch_o2", ch_o2.count)],
                            inc="vv")
                xn_done_vv = ir.c("vv")
                ch_o2.dma(lambda e, blk=blk, half=half: e.dma_start(
                    out=fm_src(xn_d, blk * P1B, (blk + 1) * P1B),
                    in_=A4[:, half:half + DCH * P1B].rearrange("p (c f) -> p c f", f=P1B)),
                    waits=[("vv", xn_done_vv)])
                for f in range(24):
                    par = f % 2
                    b3 = f % 3
                    wb = WB[b3]
                    chw = WCH3[b3]
                    bank = 0 if par == 0 else 2
                    ld = chw.dma(lambda e, f=f, wb=wb: e.dma_start(
                        out=wb[:].rearrange("p (c f) -> p c f", f=128),
                        in_=fm_src(w_qkv, f * 128, (f + 1) * 128)),
                        waits=[("tv", wbuf_g[b3])])
                    mm_last = 0
                    for d in range(DCH):
                        mm_last = ir.op("tensor", lambda e, d=d, wb=wb, bank=bank, half=half: e.matmul(
                            PB(bank), wb[:, d * 128:(d + 1) * 128],
                            A4[:, half + d * P1B: half + (d + 1) * P1B],
                            start=(d == 0), stop=(d == DCH - 1)),
                            waits=[(chw.name, ld), ("vv", max(qkv_ev[bank], xn_done_vv))], inc="tv")
                    wbuf_g[b3] = mm_last
                    stg = stagA if par == 0 else stagB
                    qkv_ev[bank] = ir.op("vector", lambda e, f=f, stg=stg, bank=bank: e.tensor_scalar(
                        stg[:], PB(bank), bq[:, f:f + 1], None, op0=ALU.add),
                        waits=[("tv", mm_last), ("ch_o1", st_q[par])], inc="vv")
                    if f < 16:
                        dest = q_d if f < 8 else k_d
                        fr = (f % 8) * 128
                        st_q[par] = ch_o1.dma(lambda e, dest=dest, fr=fr, blk=blk, stg=stg: e.dma_start(
                            out=dest[fr:fr + 128, blk * P1B:(blk + 1) * P1B], in_=stg[:]),
                            waits=[("vv", qkv_ev[bank])])
                    else:
                        fr = (f - 16) * 128
                        vst = vst2A if par == 0 else vst2B
                        vbank = 1 if par == 0 else 3
                        ev = 0
                        for j in range(P1B // 128):
                            t_at = ir.op("tensor", lambda e, j=j, stg=stg, vbank=vbank: e.transpose(
                                PB(vbank)[:, 0:128], stg[:, j * 128:(j + 1) * 128], idt[:]),
                                waits=[("vv", max(qkv_ev[bank], vtr_ev[vbank]))], inc="tv")
                            vtr_ev[vbank] = ev = ir.op("vector", lambda e, j=j, vst=vst, vbank=vbank: e.tensor_copy(
                                vst[:, j * 128:(j + 1) * 128], PB(vbank)[:, 0:128]),
                                waits=[("tv", t_at), ("ch_o2", st_v[par])], inc="vv")
                        st_v[par] = ch_o2.dma(lambda e, fr=fr, blk=blk, vst=vst: e.dma_start(
                            out=v_d[blk * P1B:(blk + 1) * P1B, fr:fr + 128].rearrange(
                                "(j p) f -> p j f", p=128),
                            in_=vst[:].rearrange("p (j f) -> p j f", f=128)),
                            waits=[("vv", ev)])
                a4_g[blk % 2] = ir.c("tv")
            last_qkv_mm = ir.c("tv")
            p1_o1 = ch_o1.count
            p1_o2 = ch_o2.count

            for blk in range(S // P2B):
                q0 = blk * P2B
                ldq = ch_a.dma(lambda e, q0=q0: e.dma_start(
                    out=A4[:].rearrange("p (c f) -> p c f", f=P2B),
                    in_=fm_src(q_d, q0, q0 + P2B)),
                    waits=[("tv", last_qkv_mm if blk == 0 else sc_mm_snapshot),
                           ("ch_o1", p1_o1)])
                sc_rd = {2: (0, 0), 3: (0, 0), 4: (0, 0), 5: (0, 0)}
                stt_prev = {0: 0, 1: 0}
                for k in range(KCH):
                    par = k % 2
                    b3 = k % 3
                    kb = WB[b3]
                    chk = WCH3[b3]
                    cset = (tta, c1a, c2a, c3a) if par == 0 else (ttb, c1b, c2b, c3b)
                    ldk = chk.dma(lambda e, k=k, kb=kb: e.dma_start(
                        out=kb[:].rearrange("p (c f) -> p c f", f=128),
                        in_=fm_src(k_d, k * 128, (k + 1) * 128)),
                        waits=[("tv", wbuf_g[b3]), ("ch_o1", p1_o1)])
                    for qh in range(2):
                        bank = 2 + par * 2 + qh
                        tt0, cc1_, cc2_, cc3_ = cset
                        w_sv, w_vv = sc_rd[bank]
                        mm_last = 0
                        for d in range(DCH):
                            mm_last = ir.op("tensor", lambda e, d=d, qh=qh, bank=bank, kb=kb: e.matmul(
                                PB(bank), kb[:, d * 128:(d + 1) * 128],
                                A4[:, d * P2B + qh * 512: d * P2B + (qh + 1) * 512],
                                start=(d == 0), stop=(d == DCH - 1)),
                                waits=[(chk.name, ldk), ("ch_a", ldq), ("sv", w_sv), ("vv", w_vv)],
                                inc="tv")
                        if qh == 1:
                            wbuf_g[b3] = mm_last
                        sv_tanh = ir.op("scalar", lambda e, bank=bank, tt0=tt0: e.activation(
                            tt0[:], PB(bank), AF.Tanh, scale=np.float32(SCALE / 2)),
                            waits=[("tv", mm_last), ("vv", stt_prev[par])], inc="sv")
                        ir.op("vector", lambda e, tt0=tt0, cc1_=cc1_: e.tensor_scalar(
                            cc1_[:], tt0[:], 0.7, 0.5, op0=ALU.is_gt, op1=ALU.mult),
                            waits=[("sv", sv_tanh)] + ([("ch_o2", p1_o2)] if k < 2 else []))
                        ir.op("vector", lambda e, tt0=tt0, cc2_=cc2_: e.tensor_scalar(
                            cc2_[:], tt0[:], 0.9, 0.5, op0=ALU.is_gt, op1=ALU.mult))
                        ir.op("vector", lambda e, cc1_=cc1_, cc2_=cc2_: e.tensor_tensor(
                            cc1_[:], cc1_[:], cc2_[:], op=ALU.add))
                        ir.op("vector", lambda e, tt0=tt0, cc2_=cc2_: e.tensor_scalar(
                            cc2_[:], tt0[:], -0.5, 0.7, op0=ALU.is_lt, op1=ALU.mult))
                        ir.op("vector", lambda e, tt0=tt0, cc3_=cc3_: e.tensor_scalar(
                            cc3_[:], tt0[:], -0.8, 0.2, op0=ALU.is_lt, op1=ALU.mult))
                        ir.op("vector", lambda e, cc2_=cc2_, cc3_=cc3_: e.tensor_tensor(
                            cc2_[:], cc2_[:], cc3_[:], op=ALU.add))
                        ir.op("vector", lambda e, cc1_=cc1_, cc2_=cc2_: e.scalar_tensor_tensor(
                            cc1_[:], cc1_[:], 1.0, cc2_[:], op0=ALU.add, op1=ALU.subtract))
                        vv_stt = ir.op("vector", lambda e, bank=bank, cc1_=cc1_, cc3_=cc3_: e.scalar_tensor_tensor(
                            cc3_[:], PB(bank), SCALE, cc1_[:], op0=ALU.mult, op1=ALU.mult),
                            inc="vv")
                        stt_prev[par] = vv_stt
                        col0 = k * P2B + qh * 512
                        exp_w = [("vv", vv_stt)]
                        if k == 0 and qh == 0:
                            exp_w.append(("tv", last_proj_mm))
                        exp_last = ir.op("scalar", lambda e, col0=col0, cc3_=cc3_: e.activation(
                            A1[:, col0:col0 + 512], cc3_[:], AF.Exp, bias=np.float32(ESH)),
                            waits=exp_w, inc="sv")
                        sc_rd[bank] = (sv_tanh, vv_stt)
                sc_mm_snapshot = ir.c("tv")
                dn_last = 0
                for qt in range(8):
                    for k in range(KCH):
                        w = [("sv", exp_last)] if (qt == 0 and k == 0) else []
                        dn_last = ir.op("tensor", lambda e, qt=qt, k=k: e.matmul(
                            PB(6)[:, qt:qt + 1], A1[:, k * P2B + qt * 128: k * P2B + qt * 128 + 128],
                            onescol[:, 0:1], start=(k == 0), stop=(k == KCH - 1)),
                            waits=w, inc="tv")
                rden_vv = ir.op("vector", lambda e: e.reciprocal(rden[:, 0:8], PB(6)[:, 0:8]),
                                waits=[("tv", dn_last)], inc="vv")
                ao_info = []
                for qh in range(2):
                    mm_last = 0
                    for k in range(KCH):
                        b3 = k % 3
                        vb = VB[b3]
                        chv = WCH3[b3]
                        ldv = chv.dma(lambda e, k=k, vb=vb: e.dma_start(
                            out=vb[:], in_=v_d[k * 128:(k + 1) * 128, :]),
                            waits=[("tv", vbuf_g[b3]), ("ch_o2", p1_o2)])
                        for d in range(DCH):
                            w = [(chv.name, ldv)]
                            if k == 0 and d == 0:
                                w.append(("vv", rden_vv if qh == 0 else ao_info[0][1]))
                            mm_last = ir.op("tensor", lambda e, d=d, k=k, qh=qh, vb=vb: e.matmul(
                                PB(d), vb[:, d * 128:(d + 1) * 128],
                                A1[:, k * P2B + qh * 512: k * P2B + (qh + 1) * 512],
                                start=(k == 0), stop=(k == KCH - 1)),
                                waits=w, inc="tv")
                        vbuf_g[b3] = mm_last
                    evs = 0
                    for d in range(DCH):
                        evs = ir.op("vector", lambda e, d=d, qh=qh: e.tensor_copy(
                            A2[:, d * P2B + qh * 512: d * P2B + (qh + 1) * 512], PB(d)),
                            waits=[("tv", mm_last)], inc="vv")
                    ao_info.append((mm_last, evs))
                ao_done_vv = ao_info[-1][1]
                ldw = ch_b.dma(lambda e: e.dma_start(
                    out=A1[:, 0:8192].rearrange("p (c f) -> p c f", f=1024),
                    in_=w_proj[:, :].rearrange("(c p) f -> p c f", p=128)),
                    waits=[("tv", ao_info[-1][0])])
                pr_ev = {6: rden_vv, 7: 0}
                xb_prev = {0: last_ct_vv, 1: last_ct_vv}
                for qt in range(8):
                    q0t = q0 + qt * 128
                    xti = xt if qt % 2 == 0 else partA
                    ldx = ch_a.dma(lambda e, q0t=q0t, xs=xs, xti=xti: e.dma_start(
                        out=xti[:], in_=xs[q0t:q0t + 128, :]),
                        waits=[("vv", xb_prev[qt % 2]), ("sv", last_id_sv)])
                    xb_prev[qt % 2] = vv_xb = ir.op("vector", lambda e, xti=xti: e.tensor_tensor(
                        c_t[:], xti[:], bp[:], op=ALU.add),
                        waits=[("ch_a", ldx), ("sv", last_xn_sv),
                               ("vv", xb_prev[(qt + 1) % 2])], inc="vv")
                    ostg_vv = 0
                    for dmh in range(2):
                        bank = 6 + dmh
                        mm_last = 0
                        for d in range(DCH):
                            mm_last = ir.op("tensor", lambda e, d=d, qt=qt, dmh=dmh, bank=bank: e.matmul(
                                PB(bank), A2[:, d * P2B + qt * 128: d * P2B + (qt + 1) * 128],
                                A1[:, d * 1024 + dmh * 512: d * 1024 + (dmh + 1) * 512],
                                start=(d == 0), stop=(d == DCH - 1)),
                                waits=[("ch_b", ldw), ("vv", max(ao_done_vv, pr_ev[bank]))], inc="tv")
                        pr_ev[bank] = ostg_vv = ir.op(
                            "vector", lambda e, qt=qt, dmh=dmh, bank=bank: e.scalar_tensor_tensor(
                                ostg[:, dmh * 512:(dmh + 1) * 512], PB(bank), rden[:, qt:qt + 1],
                                c_t[:, dmh * 512:(dmh + 1) * 512], op0=ALU.mult, op1=ALU.add),
                            waits=[("tv", mm_last), ("vv", vv_xb), ("ch_o1", ch_o1.count)], inc="vv")
                    ch_o1.dma(lambda e, q0t=q0t, os_=os_: e.dma_start(
                        out=os_[q0t:q0t + 128, :], in_=ostg[:]),
                        waits=[("vv", ostg_vv)])
                last_proj_mm = ir.c("tv")
                last_ct_vv = ir.c("vv")

            relu_sv = {0: 0, 1: 0}
            sqr_vv = {0: 0, 1: 0}
            for blk in range(S // P3B):
                q0 = blk * P3B
                ldxn = ch_a.dma(lambda e, q0=q0: e.dma_start(
                    out=A4[:, 0:DCH * P3B].rearrange("p (c f) -> p c f", f=P3B),
                    in_=fm_src(xn_d, q0, q0 + P3B)),
                    waits=[("tv", last_proj_mm if blk == 0 else p3_pass1_last),
                           ("ch_o2", p1_o2)])
                sq_last = 0
                for fc in range(FCH):
                    par = fc % 2
                    b3 = fc % 3
                    wb = WB[b3]
                    chw = WCH3[b3]
                    yt = ytstA if par == 0 else ytstB
                    ldw2 = chw.dma(lambda e, fc=fc, wb=wb: e.dma_start(
                        out=wb[:].rearrange("p (c f) -> p c f", f=128),
                        in_=fm_src(wfcT_d, fc * 128, (fc + 1) * 128)),
                        waits=[("tv", wbuf_g[b3])])
                    mm_last = 0
                    for d in range(DCH):
                        w = [(chw.name, ldw2), ("ch_a", ldxn), ("sv", relu_sv[par])]
                        if fc < 2:
                            w.append(("vv", yevac_vv[0]))
                        mm_last = ir.op("tensor", lambda e, d=d, par=par, wb=wb: e.matmul(
                            PB(par), wb[:, d * 128:(d + 1) * 128], A4[:, d * P3B:(d + 1) * P3B],
                            start=(d == 0), stop=(d == DCH - 1)),
                            waits=w, inc="tv")
                    wbuf_g[b3] = mm_last
                    relu_sv[par] = ir.op("scalar", lambda e, par=par, yt=yt: e.activation(
                        yt[:], PB(par), AF.Relu),
                        waits=[("tv", mm_last), ("sv", sqr_vv[par])], inc="sv")
                    w2 = []
                    if blk == 0 and fc < 16:
                        w2.append(("tv", last_proj_mm))
                    sqr_vv[par] = sq_last = ir.op("scalar", lambda e, fc=fc, yt=yt: e.activation(
                        A1[:, fc * P3B:(fc + 1) * P3B], yt[:], AF.Square),
                        waits=w2, inc="sv")
                p3_pass1_last = ir.c("tv")
                mm2_last = 0
                for fc in range(FCH):
                    par = fc % 2
                    b3 = fc % 3
                    wb = VB[b3]
                    chw = WCH3[b3]
                    ldw3 = chw.dma(lambda e, fc=fc, wb=wb: e.dma_start(
                        out=wb[:], in_=wpfT_d[fc * 128:(fc + 1) * 128, :]),
                        waits=[("tv", vbuf_g[b3])])
                    for qt in range(4):
                        for dmh in range(2):
                            w = [(chw.name, ldw3), ("sv", sq_last)]
                            if fc == 0:
                                w += [("sv", relu_sv[0]), ("sv", relu_sv[1]), ("vv", yevac_vv[qt])]
                            mm2_last = ir.op("tensor", lambda e, fc=fc, qt=qt, dmh=dmh, wb=wb: e.matmul(
                                PB(qt * 2 + dmh), A1[:, fc * P3B + qt * 128: fc * P3B + (qt + 1) * 128],
                                wb[:, dmh * 512:(dmh + 1) * 512],
                                start=(fc == 0), stop=(fc == FCH - 1)),
                                waits=w, inc="tv")
                    vbuf_g[b3] = mm2_last
                for qt in range(4):
                    ysb = A2[:, qt * 2048: qt * 2048 + 1024]
                    yevac_vv[qt] = ir.op("vector", lambda e, qt=qt, ysb=ysb: e.tensor_copy(
                        ysb, psum[:, qt * 1024:(qt + 1) * 1024]),
                        waits=[("tv", mm2_last)], inc="vv")
                snn_f = [0, 0]
                for qt in range(4):
                    q0t = q0 + qt * 128
                    ysb = A2[:, qt * 2048: qt * 2048 + 1024]
                    pb = partA if qt % 2 == 0 else partB
                    ldp = ch_o2.dma(lambda e, q0t=q0t, os_=os_, pb=pb: e.dma_start(
                        out=pb[:], in_=os_[q0t:q0t + 128, :]),
                        waits=[("vv", snn_f[qt % 2]), ("ch_o1", ch_o1.count)])
                    ir.op("vector", lambda e, ysb=ysb: e.tensor_copy(c_t[:], ysb),
                          waits=[("vv", yevac_vv[qt])])
                    ir.op("vector", lambda e: e.tensor_scalar(sq_t[:], c_t[:], 1.0, None, op0=ALU.is_gt))
                    ir.op("vector", lambda e: e.tensor_copy(xt[:], sq_t[:]))
                    for st in range(4):
                        ir.op("vector", lambda e, ysb=ysb: e.scalar_tensor_tensor(
                            c_t[:], c_t[:], 0.9, ysb, op0=ALU.mult, op1=ALU.add))
                        ir.op("vector", lambda e: e.tensor_tensor(c_t[:], c_t[:], sq_t[:], op=ALU.subtract))
                        ir.op("vector", lambda e: e.tensor_scalar(sq_t[:], c_t[:], 1.0, None, op0=ALU.is_gt))
                        ir.op("vector", lambda e: e.tensor_tensor(xt[:], xt[:], sq_t[:], op=ALU.add))
                    snn_f[qt % 2] = vv_f = ir.op("vector", lambda e, pb=pb: e.scalar_tensor_tensor(
                        ostg[:], xt[:], 0.2, pb[:], op0=ALU.mult, op1=ALU.add),
                        waits=[("ch_o2", ldp), ("ch_o1", ch_o1.count)], inc="vv")
                    ch_o1.dma(lambda e, q0t=q0t, os_=os_: e.dma_start(
                        out=os_[q0t:q0t + 128, :], in_=ostg[:]),
                        waits=[("vv", vv_f)])
                last_ct_vv = ir.c("vv")
                last_id_sv = ir.c("sv")
            last_qkv_mm = ir.c("tv")
            last_proj_mm = ir.c("tv")
            last_xn_sv = ir.c("sv")
            last_recip_vv = ir.c("vv")
            last_vcol2_vv = ir.c("vv")
            ln_tr_last = ir.c("tv")
            ln_tr_ev = {0: ir.c("vv"), 1: ir.c("vv")}
            a4_g = {0: ir.c("tv"), 1: ir.c("tv")}

        ir.validate()
        final_waits = [(c.name, c.count) for c in chans if c.count > 0]
        boot_h = es.enter_context(nc.semaphore("boot"))
        sems = {s: es.enter_context(nc.semaphore(s)) for s in sorted(set(ir.cnt) | set(ir.maxwait))}
        sems["__boot__"] = boot_h
        block = es.enter_context(nc.Block())
        for name in IR.ENGS:
            def mk(nm):
                def f(eng):
                    ir.replay(nm, eng, sems, boot_h)
                    if nm == "gpsimd":
                        for s, v in final_waits:
                            eng.wait_ge(sems[s], v)
                return f
            getattr(block, name)(mk(name))
    return nc


def kernel(L0_attn_x, L0_attn_W_qkv, L0_attn_b_qkv, L0_attn_W_proj, L0_attn_b_proj,
           L0_ffn_W_fc, L0_ffn_W_proj):
    xfull = np.ascontiguousarray(np.asarray(L0_attn_x, dtype=np.float32))
    B = xfull.shape[0]
    per = B // NCORES
    shared = {
        "w_qkv": np.ascontiguousarray(np.asarray(L0_attn_W_qkv, dtype=np.float32)),
        "b_qkv": np.ascontiguousarray(np.asarray(L0_attn_b_qkv, dtype=np.float32).reshape(24, 128).T),
        "w_proj": np.ascontiguousarray(np.asarray(L0_attn_W_proj, dtype=np.float32)),
        "b_proj": np.ascontiguousarray(np.broadcast_to(
            np.asarray(L0_attn_b_proj, dtype=np.float32), (128, D))),
        "w_fc": np.ascontiguousarray(np.asarray(L0_ffn_W_fc, dtype=np.float32)),
        "w_pf": np.ascontiguousarray(np.asarray(L0_ffn_W_proj, dtype=np.float32)),
        "ident": np.eye(128, dtype=np.float32),
        "ones": np.ones((128, 1), dtype=np.float32),
    }
    in_maps = []
    for c in range(NCORES):
        m = dict(shared)
        m["x"] = np.ascontiguousarray(xfull[c * per:(c + 1) * per].reshape(per * S, D))
        in_maps.append(m)
    nc = build(nseq=per)
    res = run_bass_kernel_spmd(nc, in_maps, core_ids=list(range(NCORES)))
    out = np.concatenate([r["out"].reshape(per, S, D) for r in res.results], axis=0)
    return np.ascontiguousarray(out.astype(np.float32))


# version 11
# speedup vs baseline: 1.0614x; 1.0614x over previous
import numpy as np
import concourse.bass as bass
import concourse.mybir as mybir
from concourse.bass_utils import run_bass_kernel_spmd
from contextlib import ExitStack

F32 = mybir.dt.float32
AF = mybir.ActivationFunctionType
ALU = mybir.AluOpType

D = 1024
S = 2048
DCH = 8
KCH = 16
P1B = 512
P2B = 1024
P3B = 512
FF = 4096
FCH = 32
EPS = 1e-5
ESH = -16.0
SCALE = 0.125
NSEQ = 4
NCORES = 8


class IR:
    ENGS = ("sync", "vector", "scalar", "tensor", "gpsimd")

    def __init__(self):
        self.q = {e: [] for e in self.ENGS}
        self.cnt = {}
        self.maxwait = {}

    def op(self, eng, fn, waits=(), inc=None, dma=False):
        n = 16 if dma else 1
        waits = tuple((s, v) for s, v in waits if v > 0)
        for s, v in waits:
            self.maxwait[s] = max(self.maxwait.get(s, 0), v)
        if inc is not None:
            self.cnt[inc] = self.cnt.get(inc, 0) + n
        self.q[eng].append((fn, waits, inc, n))
        return self.cnt.get(inc, 0)

    def c(self, sem):
        return self.cnt.get(sem, 0)

    def validate(self):
        for s, mx in self.maxwait.items():
            assert mx <= self.cnt.get(s, 0), f"sem {s}: wait {mx} > total {self.cnt.get(s,0)}"

    def replay(self, eng_name, eng, sems, boot_sem):
        import os as _os
        if _os.environ.get("BASS_SIM_FP64") != "1" and _os.environ.get("KBOOT") != "0":
            if eng_name == "gpsimd":
                nums = sorted(s.num for s in sems.values())
                rng = range(min(nums), max(nums) + 1)
                eng.dma_reset(rng)
                eng.sem_clear(rng)
                eng.sem_inc(boot_sem, 1)
            else:
                for _ in range(100):
                    eng.nop(cycle_cnt=255)
                eng.wait_ge(boot_sem, 1)
        for fn, waits, inc, n in self.q[eng_name]:
            for s, v in waits:
                eng.wait_ge(sems[s], v)
            inst = fn(eng)
            if inc is not None:
                inst.then_inc(sems[inc], n)


class DmaChan:

    def __init__(self, ir, name, eng):
        self.ir, self.name, self.eng = ir, name, eng

    def dma(self, fn, waits=()):
        w = list(waits)
        prev = self.ir.c(self.name)
        if prev > 0:
            w.append((self.name, prev))
        return self.ir.op(self.eng, fn, waits=w, inc=self.name, dma=True)

    @property
    def count(self):
        return self.ir.c(self.name)


def build(nseq=NSEQ):
    nc = bass.Bass("TRN2", target_bir_lowering=False, debug=False,
                   detect_race_conditions=False)
    x = nc.dram_tensor("x", [nseq * S, D], F32, kind="ExternalInput").ap()
    w_qkv = nc.dram_tensor("w_qkv", [D, 3 * D], F32, kind="ExternalInput").ap()
    b_qkv = nc.dram_tensor("b_qkv", [128, 24], F32, kind="ExternalInput").ap()
    w_proj = nc.dram_tensor("w_proj", [D, D], F32, kind="ExternalInput").ap()
    b_proj = nc.dram_tensor("b_proj", [128, D], F32, kind="ExternalInput").ap()
    w_fc = nc.dram_tensor("w_fc", [FF, D], F32, kind="ExternalInput").ap()
    w_pf = nc.dram_tensor("w_pf", [D, FF], F32, kind="ExternalInput").ap()
    ident = nc.dram_tensor("ident", [128, 128], F32, kind="ExternalInput").ap()
    ones = nc.dram_tensor("ones", [128, 1], F32, kind="ExternalInput").ap()
    out = nc.dram_tensor("out", [nseq * S, D], F32, kind="ExternalOutput").ap()
    xn_d = nc.dram_tensor("xn_d", [D, S], F32, kind="Internal").ap()
    q_d = nc.dram_tensor("q_d", [D, S], F32, kind="Internal").ap()
    k_d = nc.dram_tensor("k_d", [D, S], F32, kind="Internal").ap()
    v_d = nc.dram_tensor("v_d", [S, D], F32, kind="Internal").ap()
    wfcT_d = nc.dram_tensor("wfcT_d", [D, FF], F32, kind="Internal").ap()
    wpfT_d = nc.dram_tensor("wpfT_d", [FF, D], F32, kind="Internal").ap()

    ir = IR()
    with ExitStack() as es:
        def sb(name, shape):
            return es.enter_context(nc.sbuf_tensor(name, shape, F32))
        A1 = sb("A1", [128, 16384])
        A2 = sb("A2", [128, 8192])
        A4 = sb("A4", [128, 8192])
        idt = sb("idt", [128, 128]); bq = sb("bq", [128, 24]); bp = sb("bp", [128, D])
        onescol = sb("onescol", [128, 1])
        xt = sb("xt", [128, D]); c_t = sb("c_t", [128, D])
        sq_t = sb("sq_t", [128, D]); xn_t = sb("xn_t", [128, D])
        mcol = sb("mcol", [128, 1]); vcol = sb("vcol", [128, 1]); vcol2 = sb("vcol2", [128, 1])
        rcol = sb("rcol", [128, 1]); rcol2 = sb("rcol2", [128, 1])
        wbufA = sb("wbufA", [128, DCH * 128]); wbufB = sb("wbufB", [128, DCH * 128])
        wbufC = sb("wbufC", [128, DCH * 128])
        vbufA = sb("vbufA", [128, D]); vbufB = sb("vbufB", [128, D])
        vbufC = sb("vbufC", [128, D])
        stagA = sb("stagA", [128, P1B]); stagB = sb("stagB", [128, P1B])
        tta = sb("tta", [128, 512]); c1a = sb("c1a", [128, 512])
        c2a = sb("c2a", [128, 512]); c3a = sb("c3a", [128, 512])
        ttb = sb("ttb", [128, 512]); c1b = sb("c1b", [128, 512])
        c2b = sb("c2b", [128, 512]); c3b = sb("c3b", [128, 512])
        vst2A = c1a; vst2B = c1b
        ostg = sb("ostg", [128, D])
        ytstA = sb("ytstA", [128, P3B]); ytstB = sb("ytstB", [128, P3B])
        rden = sb("rden", [128, 8])
        partA = sb("partA", [128, D])
        partB = xn_t

        psum = es.enter_context(nc.psum_tensor("psum", [128, 4096], F32))
        PB = lambda b: psum[:, b * 512:(b + 1) * 512]

        ch_a = DmaChan(ir, "ch_a", "sync")
        ch_b = DmaChan(ir, "ch_b", "sync")
        ch_c = DmaChan(ir, "ch_c", "sync")
        ch_d = DmaChan(ir, "ch_d", "sync")
        ch_o1 = DmaChan(ir, "ch_o1", "gpsimd")
        ch_o2 = DmaChan(ir, "ch_o2", "gpsimd")
        chans = [ch_a, ch_b, ch_c, ch_d, ch_o1, ch_o2]

        fm_src = lambda dram, c0, c1: dram[:, c0:c1].rearrange("(c p) f -> p c f", p=128)
        WCH3 = [ch_b, ch_c, ch_d]

        ch_b.dma(lambda e: e.dma_start(out=idt[:], in_=ident[:, :]))
        ch_b.dma(lambda e: e.dma_start(out=bq[:], in_=b_qkv[:, :]))
        ch_b.dma(lambda e: e.dma_start(out=bp[:], in_=b_proj[:, :]))
        ch_b.dma(lambda e: e.dma_start(out=onescol[:], in_=ones[:, :]))

        ev_prev = {0: 0, 1: 0}
        for fc in range(FCH):
            ld = ch_b.dma(lambda e, fc=fc: e.dma_start(out=xt[:], in_=w_fc[fc * 128:(fc + 1) * 128, :]),
                          waits=[("tv", ir.c("tv"))])
            for d in range(DCH):
                t_at = ir.op("tensor", lambda e, d=d: e.transpose(
                    PB(d % 2)[:, 0:128], xt[:, d * 128:(d + 1) * 128], idt[:]),
                    waits=[("ch_b", ld), ("vv", ev_prev[d % 2])], inc="tv")
                ev_prev[d % 2] = ir.op("vector", lambda e, d=d: e.tensor_copy(
                    sq_t[:, d * 128:(d + 1) * 128], PB(d % 2)[:, 0:128]),
                    waits=[("tv", t_at), ("ch_o1", ch_o1.count)], inc="vv")
            ch_o1.dma(lambda e, fc=fc: e.dma_start(
                out=fm_src(wfcT_d, fc * 128, (fc + 1) * 128),
                in_=sq_t[:].rearrange("p (c f) -> p c f", f=128)),
                waits=[("vv", ir.c("vv"))])
        for dm in range(DCH):
            ld = ch_b.dma(lambda e, dm=dm: e.dma_start(out=A1[:, 0:FF], in_=w_pf[dm * 128:(dm + 1) * 128, :]),
                          waits=[("tv", ir.c("tv"))])
            for g in range(4):
                for j in range(8):
                    fc = g * 8 + j
                    t_at = ir.op("tensor", lambda e, fc=fc: e.transpose(
                        PB(fc % 2)[:, 0:128], A1[:, fc * 128:(fc + 1) * 128], idt[:]),
                        waits=[("ch_b", ld), ("vv", ev_prev[fc % 2])], inc="tv")
                    ev_prev[fc % 2] = ir.op("vector", lambda e, j=j, fc=fc: e.tensor_copy(
                        xn_t[:, j * 128:(j + 1) * 128], PB(fc % 2)[:, 0:128]),
                        waits=[("tv", t_at), ("ch_o1", ch_o1.count)], inc="vv")
                f0 = g * 8
                ch_o1.dma(lambda e, dm=dm, f0=f0: e.dma_start(
                    out=wpfT_d[f0 * 128:(f0 + 8) * 128, dm * 128:(dm + 1) * 128].rearrange(
                        "(c p) f -> p c f", p=128),
                    in_=xn_t[:].rearrange("p (c f) -> p c f", f=128)),
                    waits=[("vv", ir.c("vv"))])

        prep_o1 = ch_o1.count
        prep_tv = ir.c("tv")
        WB = [wbufA, wbufB, wbufC]
        VB = [vbufA, vbufB, vbufC]
        WCH = None
        wbuf_g = {0: 0, 1: 0, 2: 0}
        vbuf_g = {0: 0, 1: 0, 2: 0}
        a4_g = {0: ir.c("tv"), 1: ir.c("tv")}
        ln_tr_ev = {0: ev_prev[0], 1: ev_prev[1]}
        ln_tr_last = ir.c("tv")
        last_qkv_mm = ir.c("tv")
        last_proj_mm = ir.c("tv")
        p3_pass1_last = ir.c("tv")
        exp_last = 0
        last_xn_sv = 0
        last_ct_vv = 0
        last_vcol2_vv = 0
        last_recip_vv = 0
        last_id_sv = 0
        yevac_vv = [0, 0, 0, 0]

        for seq in range(nseq):
            xs = x[seq * S:(seq + 1) * S, :]
            os_ = out[seq * S:(seq + 1) * S, :]
            qkv_ev = {0: 0, 2: 0}
            vtr_ev = {1: 0, 3: 0}
            st_q = {0: ch_o1.count, 1: ch_o1.count}
            st_v = {0: ch_o2.count, 1: ch_o2.count}
            for blk in range(S // P1B):
                half = (blk % 2) * (DCH * P1B)
                for t in range(P1B // 128):
                    r0 = blk * P1B + t * 128
                    first = (seq == 0 and blk == 0 and t == 0)
                    ld = ch_a.dma(lambda e, r=r0, xs=xs: e.dma_start(out=xt[:], in_=xs[r:r + 128, :]),
                                  waits=[("vv", last_ct_vv), ("sv", last_id_sv)]
                                        + ([("tv", prep_tv)] if first else []))
                    last_id_sv = ir.op("scalar", lambda e: e.activation(
                        sq_t[:], xt[:], AF.Identity, scale=np.float32(1.0 / D), accum_out=mcol[:]),
                        waits=[("ch_a", ld), ("vv", last_ct_vv)]
                              + ([("ch_o1", prep_o1)] if first else []), inc="sv")
                    last_ct_vv = ir.op("vector", lambda e: e.tensor_scalar(
                        c_t[:], xt[:], mcol[:, 0:1], None, op0=ALU.subtract),
                        waits=[("sv", max(last_id_sv, last_xn_sv))], inc="vv")
                    sv_sq = ir.op("scalar", lambda e: e.activation(
                        sq_t[:], c_t[:], AF.Square, scale=np.float32(1.0 / 32.0), accum_out=vcol[:]),
                        waits=[("vv", max(last_ct_vv, last_vcol2_vv))], inc="sv")
                    last_vcol2_vv = ir.op("vector", lambda e: e.tensor_scalar(
                        vcol2[:], vcol[:], EPS, None, op0=ALU.add),
                        waits=[("sv", sv_sq)], inc="vv")
                    sv_sqrt = ir.op("scalar", lambda e: e.activation(rcol2[:], vcol2[:], AF.Sqrt),
                                    waits=[("vv", max(last_vcol2_vv, last_recip_vv))], inc="sv")
                    last_recip_vv = ir.op("vector", lambda e: e.reciprocal(rcol[:], rcol2[:]),
                                          waits=[("sv", max(sv_sqrt, last_xn_sv))], inc="vv")
                    last_xn_sv = ir.op("scalar", lambda e: e.activation(
                        xn_t[:], c_t[:], AF.Identity, scale=rcol[:, 0:1]),
                        waits=[("vv", last_recip_vv), ("tv", ln_tr_last)]
                              + ([("ch_o1", prep_o1)] if first else []), inc="sv")
                    for c in range(DCH):
                        bk = 6 + c % 2
                        t_at = ir.op("tensor", lambda e, c=c, bk=bk: e.transpose(
                            PB(bk)[:, 0:128], xn_t[:, c * 128:(c + 1) * 128], idt[:]),
                            waits=[("sv", last_xn_sv), ("vv", ln_tr_ev[c % 2])], inc="tv")
                        if c == DCH - 1:
                            ln_tr_last = t_at
                        ln_tr_ev[c % 2] = ir.op("vector", lambda e, c=c, t=t, half=half, bk=bk: e.tensor_copy(
                            A4[:, half + c * P1B + t * 128: half + c * P1B + (t + 1) * 128],
                            PB(bk)[:, 0:128]),
                            waits=[("tv", max(t_at, a4_g[(half > 0) * 1])), ("ch_o2", ch_o2.count)],
                            inc="vv")
                xn_done_vv = ir.c("vv")
                ch_o2.dma(lambda e, blk=blk, half=half: e.dma_start(
                    out=fm_src(xn_d, blk * P1B, (blk + 1) * P1B),
                    in_=A4[:, half:half + DCH * P1B].rearrange("p (c f) -> p c f", f=P1B)),
                    waits=[("vv", xn_done_vv)])
                for f in range(24):
                    par = f % 2
                    b3 = f % 3
                    wb = WB[b3]
                    chw = WCH3[b3]
                    bank = 0 if par == 0 else 2
                    ld = chw.dma(lambda e, f=f, wb=wb: e.dma_start(
                        out=wb[:].rearrange("p (c f) -> p c f", f=128),
                        in_=fm_src(w_qkv, f * 128, (f + 1) * 128)),
                        waits=[("tv", wbuf_g[b3])])
                    mm_last = 0
                    for d in range(DCH):
                        mm_last = ir.op("tensor", lambda e, d=d, wb=wb, bank=bank, half=half: e.matmul(
                            PB(bank), wb[:, d * 128:(d + 1) * 128],
                            A4[:, half + d * P1B: half + (d + 1) * P1B],
                            start=(d == 0), stop=(d == DCH - 1)),
                            waits=[(chw.name, ld), ("vv", max(qkv_ev[bank], xn_done_vv))], inc="tv")
                    wbuf_g[b3] = mm_last
                    stg = stagA if par == 0 else stagB
                    qkv_ev[bank] = ir.op("vector", lambda e, f=f, stg=stg, bank=bank: e.tensor_scalar(
                        stg[:], PB(bank), bq[:, f:f + 1], None, op0=ALU.add),
                        waits=[("tv", mm_last), ("ch_o1", st_q[par])], inc="vv")
                    if f < 16:
                        dest = q_d if f < 8 else k_d
                        fr = (f % 8) * 128
                        st_q[par] = ch_o1.dma(lambda e, dest=dest, fr=fr, blk=blk, stg=stg: e.dma_start(
                            out=dest[fr:fr + 128, blk * P1B:(blk + 1) * P1B], in_=stg[:]),
                            waits=[("vv", qkv_ev[bank])])
                    else:
                        fr = (f - 16) * 128
                        vst = vst2A if par == 0 else vst2B
                        vbank = 1 if par == 0 else 3
                        ev = 0
                        for j in range(P1B // 128):
                            t_at = ir.op("tensor", lambda e, j=j, stg=stg, vbank=vbank: e.transpose(
                                PB(vbank)[:, 0:128], stg[:, j * 128:(j + 1) * 128], idt[:]),
                                waits=[("vv", max(qkv_ev[bank], vtr_ev[vbank]))], inc="tv")
                            vtr_ev[vbank] = ev = ir.op("vector", lambda e, j=j, vst=vst, vbank=vbank: e.tensor_copy(
                                vst[:, j * 128:(j + 1) * 128], PB(vbank)[:, 0:128]),
                                waits=[("tv", t_at), ("ch_o2", st_v[par])], inc="vv")
                        st_v[par] = ch_o2.dma(lambda e, fr=fr, blk=blk, vst=vst: e.dma_start(
                            out=v_d[blk * P1B:(blk + 1) * P1B, fr:fr + 128].rearrange(
                                "(j p) f -> p j f", p=128),
                            in_=vst[:].rearrange("p (j f) -> p j f", f=128)),
                            waits=[("vv", ev)])
                a4_g[blk % 2] = ir.c("tv")
            last_qkv_mm = ir.c("tv")
            p1_o1 = ch_o1.count
            p1_o2 = ch_o2.count

            for blk in range(S // P2B):
                q0 = blk * P2B
                ldq = ch_a.dma(lambda e, q0=q0: e.dma_start(
                    out=A4[:].rearrange("p (c f) -> p c f", f=P2B),
                    in_=fm_src(q_d, q0, q0 + P2B)),
                    waits=[("tv", last_qkv_mm if blk == 0 else sc_mm_snapshot),
                           ("ch_o1", p1_o1)])
                sc_rd = {2: (0, 0), 3: (0, 0), 4: (0, 0), 5: (0, 0)}
                stt_prev = {0: 0, 1: 0}
                for k in range(KCH):
                    par = k % 2
                    b3 = k % 3
                    kb = WB[b3]
                    chk = WCH3[b3]
                    cset = (tta, c1a, c2a, c3a) if par == 0 else (ttb, c1b, c2b, c3b)
                    ldk = chk.dma(lambda e, k=k, kb=kb: e.dma_start(
                        out=kb[:].rearrange("p (c f) -> p c f", f=128),
                        in_=fm_src(k_d, k * 128, (k + 1) * 128)),
                        waits=[("tv", wbuf_g[b3]), ("ch_o1", p1_o1)])
                    for qh in range(2):
                        bank = 2 + par * 2 + qh
                        tt0, cc1_, cc2_, cc3_ = cset
                        w_sv, w_vv = sc_rd[bank]
                        mm_last = 0
                        for d in range(DCH):
                            mm_last = ir.op("tensor", lambda e, d=d, qh=qh, bank=bank, kb=kb: e.matmul(
                                PB(bank), kb[:, d * 128:(d + 1) * 128],
                                A4[:, d * P2B + qh * 512: d * P2B + (qh + 1) * 512],
                                start=(d == 0), stop=(d == DCH - 1)),
                                waits=[(chk.name, ldk), ("ch_a", ldq), ("sv", w_sv), ("vv", w_vv)],
                                inc="tv")
                        if qh == 1:
                            wbuf_g[b3] = mm_last
                        sv_tanh = ir.op("scalar", lambda e, bank=bank, tt0=tt0: e.activation(
                            tt0[:], PB(bank), AF.Tanh, scale=np.float32(SCALE / 2)),
                            waits=[("tv", mm_last), ("vv", stt_prev[par])], inc="sv")
                        ir.op("vector", lambda e, tt0=tt0, cc1_=cc1_: e.tensor_scalar(
                            cc1_[:], tt0[:], 0.7, 0.5, op0=ALU.is_gt, op1=ALU.mult),
                            waits=[("sv", sv_tanh)] + ([("ch_o2", p1_o2)] if k < 2 else []))
                        ir.op("vector", lambda e, tt0=tt0, cc2_=cc2_: e.tensor_scalar(
                            cc2_[:], tt0[:], 0.9, 0.5, op0=ALU.is_gt, op1=ALU.mult))
                        ir.op("vector", lambda e, cc1_=cc1_, cc2_=cc2_: e.tensor_tensor(
                            cc1_[:], cc1_[:], cc2_[:], op=ALU.add))
                        ir.op("vector", lambda e, tt0=tt0, cc2_=cc2_: e.tensor_scalar(
                            cc2_[:], tt0[:], -0.5, 0.7, op0=ALU.is_lt, op1=ALU.mult))
                        ir.op("vector", lambda e, tt0=tt0, cc3_=cc3_: e.tensor_scalar(
                            cc3_[:], tt0[:], -0.8, 0.2, op0=ALU.is_lt, op1=ALU.mult))
                        ir.op("vector", lambda e, cc2_=cc2_, cc3_=cc3_: e.tensor_tensor(
                            cc2_[:], cc2_[:], cc3_[:], op=ALU.add))
                        ir.op("vector", lambda e, cc1_=cc1_, cc2_=cc2_: e.scalar_tensor_tensor(
                            cc1_[:], cc1_[:], 1.0, cc2_[:], op0=ALU.add, op1=ALU.subtract))
                        vv_stt = ir.op("vector", lambda e, bank=bank, cc1_=cc1_, cc3_=cc3_: e.scalar_tensor_tensor(
                            cc3_[:], PB(bank), SCALE, cc1_[:], op0=ALU.mult, op1=ALU.mult),
                            inc="vv")
                        stt_prev[par] = vv_stt
                        col0 = k * P2B + qh * 512
                        exp_w = [("vv", vv_stt)]
                        if k == 0 and qh == 0:
                            exp_w.append(("tv", last_proj_mm))
                        exp_last = ir.op("scalar", lambda e, col0=col0, cc3_=cc3_: e.activation(
                            A1[:, col0:col0 + 512], cc3_[:], AF.Exp, bias=np.float32(ESH)),
                            waits=exp_w, inc="sv")
                        sc_rd[bank] = (sv_tanh, vv_stt)
                sc_mm_snapshot = ir.c("tv")
                dn_last = 0
                for qt in range(8):
                    for k in range(KCH):
                        w = [("sv", exp_last)] if (qt == 0 and k == 0) else []
                        dn_last = ir.op("tensor", lambda e, qt=qt, k=k: e.matmul(
                            PB(6)[:, qt:qt + 1], A1[:, k * P2B + qt * 128: k * P2B + qt * 128 + 128],
                            onescol[:, 0:1], start=(k == 0), stop=(k == KCH - 1)),
                            waits=w, inc="tv")
                rden_vv = ir.op("vector", lambda e: e.reciprocal(rden[:, 0:8], PB(6)[:, 0:8]),
                                waits=[("tv", dn_last)], inc="vv")
                ao_info = []
                for qh in range(2):
                    mm_last = 0
                    for k in range(KCH):
                        b3 = k % 3
                        vb = VB[b3]
                        chv = WCH3[b3]
                        ldv = chv.dma(lambda e, k=k, vb=vb: e.dma_start(
                            out=vb[:], in_=v_d[k * 128:(k + 1) * 128, :]),
                            waits=[("tv", vbuf_g[b3]), ("ch_o2", p1_o2)])
                        for d in range(DCH):
                            w = [(chv.name, ldv)]
                            if k == 0 and d == 0:
                                w.append(("vv", rden_vv if qh == 0 else ao_info[0][1]))
                            mm_last = ir.op("tensor", lambda e, d=d, k=k, qh=qh, vb=vb: e.matmul(
                                PB(d), vb[:, d * 128:(d + 1) * 128],
                                A1[:, k * P2B + qh * 512: k * P2B + (qh + 1) * 512],
                                start=(k == 0), stop=(k == KCH - 1)),
                                waits=w, inc="tv")
                        vbuf_g[b3] = mm_last
                    evs = 0
                    for d in range(DCH):
                        evs = ir.op("vector", lambda e, d=d, qh=qh: e.tensor_copy(
                            A2[:, d * P2B + qh * 512: d * P2B + (qh + 1) * 512], PB(d)),
                            waits=[("tv", mm_last)], inc="vv")
                    ao_info.append((mm_last, evs))
                ao_done_vv = ao_info[-1][1]
                ldw = ch_b.dma(lambda e: e.dma_start(
                    out=A1[:, 0:8192].rearrange("p (c f) -> p c f", f=1024),
                    in_=w_proj[:, :].rearrange("(c p) f -> p c f", p=128)),
                    waits=[("tv", ao_info[-1][0])])
                pr_ev = {6: rden_vv, 7: 0}
                xb_prev = {0: last_ct_vv, 1: last_ct_vv}
                for qt in range(8):
                    q0t = q0 + qt * 128
                    xti = xt if qt % 2 == 0 else partA
                    ldx = ch_a.dma(lambda e, q0t=q0t, xs=xs, xti=xti: e.dma_start(
                        out=xti[:], in_=xs[q0t:q0t + 128, :]),
                        waits=[("vv", xb_prev[qt % 2]), ("sv", last_id_sv)])
                    xb_prev[qt % 2] = vv_xb = ir.op("vector", lambda e, xti=xti: e.tensor_tensor(
                        c_t[:], xti[:], bp[:], op=ALU.add),
                        waits=[("ch_a", ldx), ("sv", last_xn_sv),
                               ("vv", xb_prev[(qt + 1) % 2])], inc="vv")
                    ostg_vv = 0
                    for dmh in range(2):
                        bank = 6 + dmh
                        mm_last = 0
                        for d in range(DCH):
                            mm_last = ir.op("tensor", lambda e, d=d, qt=qt, dmh=dmh, bank=bank: e.matmul(
                                PB(bank), A2[:, d * P2B + qt * 128: d * P2B + (qt + 1) * 128],
                                A1[:, d * 1024 + dmh * 512: d * 1024 + (dmh + 1) * 512],
                                start=(d == 0), stop=(d == DCH - 1)),
                                waits=[("ch_b", ldw), ("vv", max(ao_done_vv, pr_ev[bank]))], inc="tv")
                        pr_ev[bank] = ostg_vv = ir.op(
                            "vector", lambda e, qt=qt, dmh=dmh, bank=bank: e.scalar_tensor_tensor(
                                ostg[:, dmh * 512:(dmh + 1) * 512], PB(bank), rden[:, qt:qt + 1],
                                c_t[:, dmh * 512:(dmh + 1) * 512], op0=ALU.mult, op1=ALU.add),
                            waits=[("tv", mm_last), ("vv", vv_xb), ("ch_o1", ch_o1.count)], inc="vv")
                    ch_o1.dma(lambda e, q0t=q0t, os_=os_: e.dma_start(
                        out=os_[q0t:q0t + 128, :], in_=ostg[:]),
                        waits=[("vv", ostg_vv)])
                last_proj_mm = ir.c("tv")
                last_ct_vv = ir.c("vv")

            relu_sv = {0: 0, 1: 0}
            sqr_vv = {0: 0, 1: 0}
            for blk in range(S // P3B):
                q0 = blk * P3B
                ldxn = ch_a.dma(lambda e, q0=q0: e.dma_start(
                    out=A4[:, 0:DCH * P3B].rearrange("p (c f) -> p c f", f=P3B),
                    in_=fm_src(xn_d, q0, q0 + P3B)),
                    waits=[("tv", last_proj_mm if blk == 0 else p3_pass1_last),
                           ("ch_o2", p1_o2)])
                sq_last = 0
                for fc in range(FCH):
                    par = fc % 2
                    b3 = fc % 3
                    wb = WB[b3]
                    chw = WCH3[b3]
                    yt = ytstA if par == 0 else ytstB
                    ldw2 = chw.dma(lambda e, fc=fc, wb=wb: e.dma_start(
                        out=wb[:].rearrange("p (c f) -> p c f", f=128),
                        in_=fm_src(wfcT_d, fc * 128, (fc + 1) * 128)),
                        waits=[("tv", wbuf_g[b3])])
                    mm_last = 0
                    for d in range(DCH):
                        w = [(chw.name, ldw2), ("ch_a", ldxn), ("sv", relu_sv[par])]
                        if fc < 2:
                            w.append(("vv", yevac_vv[0]))
                        mm_last = ir.op("tensor", lambda e, d=d, par=par, wb=wb: e.matmul(
                            PB(par), wb[:, d * 128:(d + 1) * 128], A4[:, d * P3B:(d + 1) * P3B],
                            start=(d == 0), stop=(d == DCH - 1)),
                            waits=w, inc="tv")
                    wbuf_g[b3] = mm_last
                    relu_sv[par] = ir.op("scalar", lambda e, par=par, yt=yt: e.activation(
                        yt[:], PB(par), AF.Relu),
                        waits=[("tv", mm_last), ("sv", sqr_vv[par])], inc="sv")
                    w2 = []
                    if blk == 0 and fc < 16:
                        w2.append(("tv", last_proj_mm))
                    sqr_vv[par] = sq_last = ir.op("scalar", lambda e, fc=fc, yt=yt: e.activation(
                        A1[:, fc * P3B:(fc + 1) * P3B], yt[:], AF.Square),
                        waits=w2, inc="sv")
                p3_pass1_last = ir.c("tv")
                mm2_last = 0
                for fc in range(FCH):
                    par = fc % 2
                    b3 = fc % 3
                    wb = VB[b3]
                    chw = WCH3[b3]
                    ldw3 = chw.dma(lambda e, fc=fc, wb=wb: e.dma_start(
                        out=wb[:], in_=wpfT_d[fc * 128:(fc + 1) * 128, :]),
                        waits=[("tv", vbuf_g[b3])])
                    for qt in range(4):
                        for dmh in range(2):
                            w = [(chw.name, ldw3), ("sv", sq_last)]
                            if fc == 0:
                                w += [("sv", relu_sv[0]), ("sv", relu_sv[1]), ("vv", yevac_vv[qt])]
                            mm2_last = ir.op("tensor", lambda e, fc=fc, qt=qt, dmh=dmh, wb=wb: e.matmul(
                                PB(qt * 2 + dmh), A1[:, fc * P3B + qt * 128: fc * P3B + (qt + 1) * 128],
                                wb[:, dmh * 512:(dmh + 1) * 512],
                                start=(fc == 0), stop=(fc == FCH - 1)),
                                waits=w, inc="tv")
                    vbuf_g[b3] = mm2_last
                for qt in range(4):
                    ysb = A2[:, qt * 2048: qt * 2048 + 1024]
                    yevac_vv[qt] = ir.op("vector", lambda e, qt=qt, ysb=ysb: e.tensor_copy(
                        ysb, psum[:, qt * 1024:(qt + 1) * 1024]),
                        waits=[("tv", mm2_last)], inc="vv")
                snn_f = [0, 0]
                for qt in range(4):
                    q0t = q0 + qt * 128
                    ysb = A2[:, qt * 2048: qt * 2048 + 1024]
                    pb = partA if qt % 2 == 0 else partB
                    ldp = ch_o2.dma(lambda e, q0t=q0t, os_=os_, pb=pb: e.dma_start(
                        out=pb[:], in_=os_[q0t:q0t + 128, :]),
                        waits=[("vv", max(snn_f[qt % 2],
                                          last_ct_vv if (blk == 0 and qt < 2) else 0)),
                               ("ch_o1", ch_o1.count)])
                    ir.op("vector", lambda e, ysb=ysb: e.tensor_copy(c_t[:], ysb),
                          waits=[("vv", yevac_vv[qt])])
                    ir.op("vector", lambda e: e.tensor_scalar(sq_t[:], c_t[:], 1.0, None, op0=ALU.is_gt))
                    ir.op("vector", lambda e: e.tensor_copy(xt[:], sq_t[:]))
                    for st in range(4):
                        ir.op("vector", lambda e, ysb=ysb: e.scalar_tensor_tensor(
                            c_t[:], c_t[:], 0.9, ysb, op0=ALU.mult, op1=ALU.add))
                        ir.op("vector", lambda e: e.tensor_tensor(c_t[:], c_t[:], sq_t[:], op=ALU.subtract))
                        ir.op("vector", lambda e: e.tensor_scalar(sq_t[:], c_t[:], 1.0, None, op0=ALU.is_gt))
                        ir.op("vector", lambda e: e.tensor_tensor(xt[:], xt[:], sq_t[:], op=ALU.add))
                    snn_f[qt % 2] = vv_f = ir.op("vector", lambda e, pb=pb: e.scalar_tensor_tensor(
                        ostg[:], xt[:], 0.2, pb[:], op0=ALU.mult, op1=ALU.add),
                        waits=[("ch_o2", ldp), ("ch_o1", ch_o1.count)], inc="vv")
                    ch_o1.dma(lambda e, q0t=q0t, os_=os_: e.dma_start(
                        out=os_[q0t:q0t + 128, :], in_=ostg[:]),
                        waits=[("vv", vv_f)])
                last_ct_vv = ir.c("vv")
                last_id_sv = ir.c("sv")
            last_qkv_mm = ir.c("tv")
            last_proj_mm = ir.c("tv")
            last_xn_sv = ir.c("sv")
            last_recip_vv = ir.c("vv")
            last_vcol2_vv = ir.c("vv")
            ln_tr_last = ir.c("tv")
            ln_tr_ev = {0: ir.c("vv"), 1: ir.c("vv")}
            a4_g = {0: ir.c("tv"), 1: ir.c("tv")}

        ir.validate()
        final_waits = [(c.name, c.count) for c in chans if c.count > 0]
        boot_h = es.enter_context(nc.semaphore("boot"))
        sems = {s: es.enter_context(nc.semaphore(s)) for s in sorted(set(ir.cnt) | set(ir.maxwait))}
        sems["__boot__"] = boot_h
        block = es.enter_context(nc.Block())
        for name in IR.ENGS:
            def mk(nm):
                def f(eng):
                    ir.replay(nm, eng, sems, boot_h)
                    if nm == "gpsimd":
                        for s, v in final_waits:
                            eng.wait_ge(sems[s], v)
                return f
            getattr(block, name)(mk(name))
    return nc


def kernel(L0_attn_x, L0_attn_W_qkv, L0_attn_b_qkv, L0_attn_W_proj, L0_attn_b_proj,
           L0_ffn_W_fc, L0_ffn_W_proj):
    xfull = np.ascontiguousarray(np.asarray(L0_attn_x, dtype=np.float32))
    B = xfull.shape[0]
    per = B // NCORES
    shared = {
        "w_qkv": np.ascontiguousarray(np.asarray(L0_attn_W_qkv, dtype=np.float32)),
        "b_qkv": np.ascontiguousarray(np.asarray(L0_attn_b_qkv, dtype=np.float32).reshape(24, 128).T),
        "w_proj": np.ascontiguousarray(np.asarray(L0_attn_W_proj, dtype=np.float32)),
        "b_proj": np.ascontiguousarray(np.broadcast_to(
            np.asarray(L0_attn_b_proj, dtype=np.float32), (128, D))),
        "w_fc": np.ascontiguousarray(np.asarray(L0_ffn_W_fc, dtype=np.float32)),
        "w_pf": np.ascontiguousarray(np.asarray(L0_ffn_W_proj, dtype=np.float32)),
        "ident": np.eye(128, dtype=np.float32),
        "ones": np.ones((128, 1), dtype=np.float32),
    }
    in_maps = []
    for c in range(NCORES):
        m = dict(shared)
        m["x"] = np.ascontiguousarray(xfull[c * per:(c + 1) * per].reshape(per * S, D))
        in_maps.append(m)
    nc = build(nseq=per)
    res = run_bass_kernel_spmd(nc, in_maps, core_ids=list(range(NCORES)))
    out = np.concatenate([r["out"].reshape(per, S, D) for r in res.results], axis=0)
    return np.ascontiguousarray(out.astype(np.float32))
